# version 05 (final)
# baseline (speedup 1.0000x reference)
import numpy as np
import ml_dtypes
from contextlib import ExitStack

import concourse.bass as bass
import concourse.mybir as mybir
from concourse.bass_utils import run_bass_kernel_spmd

F32 = mybir.dt.float32
BF16 = mybir.dt.bfloat16
AF = mybir.ActivationFunctionType
ALU = mybir.AluOpType
AX = mybir.AxisListType

ROUTE_SCALE = 2.826
LN_EPS = 1e-5
GN_EPS = 1e-5 * 64.0
NEG_BIG = -1e30

SERIAL = False


class TGuard:

    def __init__(self, b, eng):
        self._b = b
        self._e = eng

    def __getattr__(self, name):
        return getattr(self._e, name)

    def _guard(self, sig):
        b = self._b
        if b.mm_rows is not None and sig != b.mm_rows and b.mm_tok is not None:
            b.wait(self._e, [b.mm_tok])
        b.mm_pending = sig

    def matmul(self, out, lhsT, rhs, **kw):
        self._guard(("mm", lhsT.base_partition(), lhsT.partition_size()))
        return self._e.matmul(out, lhsT, rhs, **kw)

    def transpose(self, out, in_, ident, **kw):
        self._guard(("tr", in_.base_partition(), in_.partition_size()))
        return self._e.transpose(out, in_, ident, **kw)


def _psum_banks(args, kw):
    banks = set()
    for a in list(args) + list(kw.values()):
        try:
            if str(getattr(a, "space", "")).endswith("PSUM"):
                banks.add(a.tensor.name)
        except Exception:
            pass
    return banks


class EGuard:

    def __init__(self, b, eng):
        self._b = b
        self._e = eng

    def __getattr__(self, name):
        real = getattr(self._e, name)
        if not callable(real) or name in ("wait_ge", "dma_start",
                                          "dma_start_transpose"):
            return real
        bsel = self

        def wrap(*args, **kw):
            b = bsel._b
            banks = _psum_banks(args, kw)
            for bk in banks:
                last = b.bank_last.get(bk)
                if last is not None and last[0] != bsel._e.engine and \
                        last[1] is not None:
                    b.wait(bsel._e, [last[1]])
            if banks:
                b.eng_pending = (bsel._e.engine, banks)
            return real(*args, **kw)
        return wrap
ATTN_TRUNC = 7
PH1_TRUNC = 5
FFN_TRUNC = 7


def full_cfg():
    return dict(T=2048, C=1024, E=16, KTOP=8, I=1024, ISH=1024, HS=64, DM=32,
                DD=64, LBLK=64, TFN=256, TAT=256, NSLOT=5)


class Builder:
    def __init__(self, cfg):
        self.cfg = cfg
        self.nc = bass.Bass("TRN2", target_bir_lowering=False)
        c = cfg
        self.T, self.C, self.E, self.I, self.ISH = c["T"], c["C"], c["E"], c["I"], c["ISH"]
        self.KTOP, self.HS, self.DM, self.DD = c["KTOP"], c["HS"], c["DM"], c["DD"]
        self.LBLK, self.TFN, self.TAT, self.NSLOT = c["LBLK"], c["TFN"], c["TAT"], c["NSLOT"]
        self.NCC = self.C // 128
        self.NI = self.I // 128
        self.NISH = self.ISH // 128
        self.H = self.C // self.HS
        self.HPC = 128 // self.HS
        self.NB = self.TAT // self.LBLK
        self.NTS = self.TAT // 128
        self.NTC = self.T // self.TFN
        self.NQ = self.T // self.TAT
        self.NSF = self.TFN // 128
        self.sems = {}
        self.cnt = {}
        self.mark = {}
        self.mm_rows = None
        self.mm_tok = None
        self.mm_pending = None
        self.bank_last = {}
        self.eng_pending = None

    def sem(self, name):
        if name not in self.sems:
            self.sems[name] = self.ctx.enter_context(self.nc.semaphore(name))
            self.cnt[name] = 0
        return self.sems[name]

    def all_toks(self):
        return [(nm, self.cnt[nm]) for nm in list(self.cnt)]

    def wait(self, eng, toks):
        best = {}
        for semname, val in toks:
            if val <= 0:
                continue
            assert semname in self.cnt and val <= self.cnt[semname], \
                (semname, val, self.cnt.get(semname))
            if val > best.get(semname, 0):
                best[semname] = val
        for semname, val in best.items():
            k = (eng.engine, semname)
            if self.mark.get(k, 0) >= val:
                continue
            eng.wait_ge(self.sems[semname], val)
            self.mark[k] = val

    def op(self, eng, semname, f, waits=()):
        self.sem(semname)
        if SERIAL:
            waits = list(waits) + self.all_toks()
        self.wait(eng, waits)
        inst = f()
        self.cnt[semname] += 1
        inst.then_inc(self.sems[semname], 1)
        tok = (semname, self.cnt[semname])
        if self.mm_pending is not None:
            self.mm_rows = self.mm_pending
            self.mm_tok = tok
            self.mm_pending = None
        if self.eng_pending is not None:
            engname, banks = self.eng_pending
            for bk in banks:
                self.bank_last[bk] = (engname, tok)
            self.eng_pending = None
        return tok

    def opn(self, eng, f, waits=()):
        self.op(eng, "pe_s", f, waits)
        return None

    def dma(self, eng, semname, out, in_, waits=()):
        self.sem(semname)
        if SERIAL:
            waits = list(waits) + self.all_toks()
        self.wait(eng, waits)
        inst = eng.dma_start(out, in_)
        self.cnt[semname] += 16
        inst.then_inc(self.sems[semname], 16)
        return (semname, self.cnt[semname])

    def dma_tr(self, eng, semname, out, in_, waits=()):
        self.sem(semname)
        if SERIAL:
            waits = list(waits) + self.all_toks()
        self.wait(eng, waits)
        inst = eng.dma_start_transpose(out, in_)
        self.cnt[semname] += 16
        inst.then_inc(self.sems[semname], 16)
        return (semname, self.cnt[semname])

    @property
    def t(self):
        if not hasattr(self, "_tguard"):
            self._tguard = TGuard(self, self.nc.tensor)
        return self._tguard
    @property
    def v(self):
        if not hasattr(self, "_vguard"):
            self._vguard = EGuard(self, self.nc.vector)
        return self._vguard

    @property
    def s(self):
        if not hasattr(self, "_sguard"):
            self._sguard = EGuard(self, self.nc.scalar)
        return self._sguard

    @property
    def g(self):
        if not hasattr(self, "_gguard"):
            self._gguard = EGuard(self, self.nc.gpsimd)
        return self._gguard
    @property
    def y(self): return self.nc.sync

    def build(self):
        nc = self.nc
        T, C, E, I, ISH = self.T, self.C, self.E, self.I, self.ISH
        DM, DD, H, HS = self.DM, self.DD, self.H, self.HS
        din = {}
        def DIN(name, shape, dtype=F32):
            din[name] = nc.dram_tensor(name, list(shape), dtype, kind="ExternalInput").ap()
        DIN("x", (T, C))
        for nm in ["maa_x", "maa_w", "maa_k", "maa_v", "maa_r", "maa_g",
                   "tdecay", "ln_w", "ln_b"]:
            DIN(nm, (C,))
        DIN("maa_w1", (C, DM * 5)); DIN("maa_w2", (5, DM, C))
        DIN("tdw1", (C, DD)); DIN("tdw2", (DD, C))
        DIN("faaaa", (H, HS))
        for nm in ["Wr", "Wk", "Wv", "Wg", "Wo"]:
            DIN(nm, (C, C))
        DIN("gate_w", (E, C)); DIN("ebias", (E,))
        DIN("w_gate", (E, I, C)); DIN("w_up", (E, I, C)); DIN("w_down", (E, C, I))
        DIN("shg", (ISH, C)); DIN("shu", (ISH, C)); DIN("shd", (C, ISH))
        self.din = din
        self.out = nc.dram_tensor("out", [T, C], F32, kind="ExternalOutput").ap()
        self.xres = nc.dram_tensor("xres", [C, T], F32, kind="Internal").ap()
        self.wT = {
            "g": nc.dram_tensor("wgT", [E, C, I], BF16, kind="Internal").ap(),
            "u": nc.dram_tensor("wuT", [E, C, I], BF16, kind="Internal").ap(),
            "d": nc.dram_tensor("wdT", [E, I, C], BF16, kind="Internal").ap(),
            "sg": nc.dram_tensor("shgT", [C, ISH], BF16, kind="Internal").ap(),
            "su": nc.dram_tensor("shuT", [C, ISH], BF16, kind="Internal").ap(),
            "sd": nc.dram_tensor("shdT", [ISH, C], BF16, kind="Internal").ap(),
        }

        with ExitStack() as ctx:
            self.ctx = ctx
            self.sb = {}
            self.alloc_longlived()
            self.ps = [ctx.enter_context(nc.psum_tensor(f"pb{i}", [128, 512], F32))
                       for i in range(8)]
            blk = ctx.enter_context(nc.Block())

            @blk.sync
            def _(_sy):
                with nc.allow_low_precision(
                        reason="bf16 matmul operands; deliberate fp32->bf16 casts"):
                    self.emit_all()
        return nc

    def A(self, name, shape, dtype, ctx=None):
        (ctx or self.ctx).enter_context
        t = (ctx or self.ctx).enter_context(self.nc.sbuf_tensor(name, list(shape), dtype))
        self.sb[name] = t
        return t

    def alloc_longlived(self):
        C, E, DM, DD, H = self.C, self.E, self.DM, self.DD, self.H
        NCC, NSLOT = self.NCC, self.NSLOT
        self.A("ident_b", (128, 128), BF16)
        self.A("ident_f", (128, 128), F32)
        self.A("ones_col", (128, 1), F32)
        self.A("ones_row_f", (1, 128), F32)
        self.A("ones_row_b", (1, 128), BF16)
        self.A("ones2_b", (128, self.HPC), BF16)
        self.A("mask_b", (128, self.LBLK), F32)
        self.A("lnw_bc", (128, C), F32)
        self.A("lnb_bc", (128, C), F32)
        self.A("ebias_bc", (128, E), F32)
        self.A("maacol", (128, 6 * NCC), F32)
        self.A("tdcol", (128, NCC), F32)
        self.A("ucol", (128, NCC), F32)
        self.A("gwT", (128, NCC * E), BF16)
        self.A("w1sb", (128, NCC * DM * 5), BF16)
        self.A("w2sb", (96, 2 * C), BF16)
        self.A("zcol", (128, 1), F32)
        self.A("cvals", (128, 4), F32)
        self.A("dw1sb", (128, NCC * DD), BF16)
        self.A("dw2sb", (DD, C), BF16)
        self.A("rowbuf", (1, 4 * max(self.TAT, self.TFN)), F32)

    def emit_all(self):
        self.consts_phase()
        self.phase_barrier()
        self.xin_phase()
        self.phase_barrier()
        self.ffn_phase(call=0)
        self.attn_phase()
        self.ffn_phase(call=1)
        self.xout_phase()

    def consts_phase(self):
        sb, din = self.sb, self.din
        g, v, t, s = self.g, self.v, self.t, self.s
        C, E, DM, DD, NCC, HS, HPC, LBLK = (self.C, self.E, self.DM, self.DD,
                                            self.NCC, self.HS, self.HPC, self.LBLK)
        tk1 = self.op(g, "pool", lambda: g.memset(sb["ident_b"][:], 0.0))
        tk2 = self.op(g, "pool", lambda: g.memset(sb["ident_f"][:], 0.0))
        self.op(g, "pool", lambda: g.memset(sb["ones_col"][:], 1.0 / C))
        self.op(g, "pool", lambda: g.memset(sb["ones_row_f"][:], 1.0))
        self.op(g, "pool", lambda: g.memset(sb["zcol"][:], 0.0))
        self.op(g, "pool", lambda: g.memset(sb["cvals"][:, 0:1], LN_EPS))
        self.op(g, "pool", lambda: g.memset(sb["cvals"][:, 1:2], GN_EPS))
        self.op(g, "pool", lambda: g.memset(sb["cvals"][:, 2:3], -1.0))
        self.op(g, "pool", lambda: g.memset(sb["cvals"][:, 3:4], 1.0))
        for val, col in [(LN_EPS, 0), (GN_EPS, 1), (-1.0, 2), (1.0, 3)]:
            self.nc.const_aps.register(F32, val, sb["cvals"][:, col:col + 1])
        self.op(g, "pool", lambda: g.memset(sb["ones_row_b"][:], 1.0))
        tk3 = self.op(g, "pool", lambda: g.memset(sb["ones2_b"][:], 1.0))
        tk4 = self.op(g, "pool", lambda: g.memset(sb["mask_b"][:], 1.0))
        t_idb = self.op(g, "pool", lambda: g.affine_select(
            sb["ident_b"][:], sb["ident_b"][:], [[1, 128]], ALU.not_equal, 1.0,
            base=0, channel_multiplier=-1), waits=[tk1])
        t_idf = self.op(g, "pool", lambda: g.affine_select(
            sb["ident_f"][:], sb["ident_f"][:], [[1, 128]], ALU.not_equal, 1.0,
            base=0, channel_multiplier=-1), waits=[tk2])
        t_msk0 = self.op(g, "pool", lambda: g.affine_select(
            sb["mask_b"][0:LBLK, :], sb["mask_b"][0:LBLK, :], [[1, LBLK]],
            ALU.is_gt, 0.0, base=0, channel_multiplier=-1), waits=[tk4])
        t_msk = self.dma(g, "d_cn", sb["mask_b"][LBLK:2 * LBLK, :],
                         sb["mask_b"][0:LBLK, :], waits=[t_msk0])
        t_o2a = self.op(g, "pool", lambda: g.affine_select(
            sb["ones2_b"][:], sb["ones2_b"][:], [[-HS, HPC]], ALU.is_ge, 0.0,
            base=0, channel_multiplier=1), waits=[tk3])
        t_o2 = self.op(g, "pool", lambda: g.affine_select(
            sb["ones2_b"][:], sb["ones2_b"][:], [[HS, HPC]], ALU.is_ge, 0.0,
            base=HS - 1, channel_multiplier=-1), waits=[t_o2a])
        self.t_idb, self.t_idf, self.t_msk, self.t_o2 = t_idb, t_idf, t_msk, t_o2

        def fm_col(dst, col, vec):
            for c in range(NCC):
                self.dma(g, "d_cn", dst[:, col * NCC + c: col * NCC + c + 1],
                         vec[c * 128:(c + 1) * 128].unsqueeze(1))
        for i, nm in enumerate(["maa_x", "maa_w", "maa_k", "maa_v", "maa_r", "maa_g"]):
            fm_col(sb["maacol"], i, din[nm])
        fm_col(sb["tdcol"], 0, din["tdecay"])
        fm_col(sb["ucol"], 0, din["faaaa"].rearrange("h i -> (h i)"))
        for c in range(NCC):
            self.dma(g, "d_cn", sb["w1sb"][:, c * DM * 5:(c + 1) * DM * 5],
                     din["maa_w1"][c * 128:(c + 1) * 128, :])
            self.dma(g, "d_cn", sb["dw1sb"][:, c * DD:(c + 1) * DD],
                     din["tdw1"][c * 128:(c + 1) * 128, :])
        for si in range(5):
            self.dma(g, "d_cn",
                     sb["w2sb"][(si % 3) * 32:(si % 3) * 32 + DM,
                                (si // 3) * C:(si // 3) * C + C],
                     din["maa_w2"][si])
        self.dma(g, "d_cn", sb["dw2sb"][:], din["tdw2"][:])
        with self.nc.sbuf_tensor("gstage", [E, C], BF16) as gstage, \
             self.nc.sbuf_tensor("lnwrow", [1, C], F32) as lnwrow, \
             self.nc.sbuf_tensor("lnbrow", [1, C], F32) as lnbrow, \
             self.nc.sbuf_tensor("ebrow", [1, E], F32) as ebrow:
            sb["lnwrow"], sb["lnbrow"], sb["ebrow"] = lnwrow, lnbrow, ebrow
            self.dma(g, "d_cn", lnwrow[:], din["ln_w"].unsqueeze(0))
            self.dma(g, "d_cn", lnbrow[:], din["ln_b"].unsqueeze(0))
            self.dma(g, "d_cn", ebrow[:], din["ebias"].unsqueeze(0))
            tload = self.dma(g, "d_cn", gstage[:], din["gate_w"][:])
            prev_ev = None
            for c in range(NCC):
                w = [tload, t_idb] + ([prev_ev] if prev_ev else [])
                p7b = self.ps[7][:].bitcast(BF16)
                tk = self.op(t, "pe", lambda c=c, p7b=p7b: t.transpose(
                    p7b[0:128, 0:E], gstage[0:E, c * 128:(c + 1) * 128],
                    sb["ident_b"][0:E, 0:E]), waits=w)
                prev_ev = self.op(v, "dve", lambda c=c, p7b=p7b: v.tensor_copy(
                    sb["gwT"][:, c * E:(c + 1) * E], p7b[0:128, 0:E]), waits=[tk])
            self.tok_gwT = prev_ev
            def bcast(dst, row, n, extra=()):
                ev = None
                for n0 in range(0, n, 512):
                    nn = min(512, n - n0)
                    tk = self.op(t, "pe", lambda n0=n0, nn=nn: t.matmul(
                        self.ps[7][:, 0:nn], sb["ones_row_f"][:], row[0:1, n0:n0 + nn],
                        start=True, stop=True),
                        waits=[t_idf, prev_ev] + list(extra) + ([ev] if ev else []))
                    ev = self.op(v, "dve", lambda n0=n0, nn=nn: v.tensor_copy(
                        dst[:, n0:n0 + nn], self.ps[7][:, 0:nn]), waits=[tk])
                return ev
            e1 = bcast(sb["lnw_bc"], sb["lnwrow"][0:1, 0:C], C)
            e2 = bcast(sb["lnb_bc"], sb["lnbrow"][0:1, 0:C], C, extra=[e1])
            e3 = bcast(sb["ebias_bc"], sb["ebrow"][0:1, 0:E], E, extra=[e2])
            self.tok_consts = e3
            self.const_dma_tok = ("d_cn", self.cnt.get("d_cn", 0))

    def xin_phase(self):
        sb, ps = self.sb, self.ps
        g, v, t, s, y = self.g, self.v, self.t, self.s, self.y
        T, C, NCC = self.T, self.C, self.NCC
        with self.nc.sbuf_tensor("stg0", [128, C], F32) as s0, \
             self.nc.sbuf_tensor("stg1", [128, C], F32) as s1, \
             self.nc.sbuf_tensor("fmas0", [128, NCC * 128], F32) as f0, \
             self.nc.sbuf_tensor("fmas1", [128, NCC * 128], F32) as f1:
            stg = [s0, s1]
            fma = [f0, f1]
            ld_tok = [None, None]
            ev_tok = [[], []]
            st_tok = [None, None]
            fma_rd = [None, None]
            self.xres_tok = []
            ntile = T // 128
            for i in range(ntile):
                p = i % 2
                w = [self.const_dma_tok]
                if ld_tok[p] is not None:
                    w += ev_tok[p]
                ld_tok[p] = self.dma(y, f"d_xin{p}", stg[p][:],
                                     self.din["x"][i * 128:(i + 1) * 128, :], waits=w)
                ev_tok[p] = []
                for c in range(NCC):
                    bank = 6 + (c % 2)
                    wl = [ld_tok[p], self.t_idf]
                    if st_tok[p] is not None:
                        wl.append(st_tok[p])
                    tk = self.op(t, "pe", lambda c=c, bank=bank: t.transpose(
                        ps[bank][:, 0:128], stg[p][:, c * 128:(c + 1) * 128],
                        sb["ident_f"][:]), waits=wl + ([fma_rd[p]] if fma_rd[p] else []))
                    ev = self.op(v, "dve", lambda c=c, bank=bank: v.tensor_copy(
                        fma[p][:, c * 128:(c + 1) * 128], ps[bank][:, 0:128]),
                        waits=[tk])
                    ev_tok[p].append(ev)
                    self.wait(t, [ev])
                dst = bass.AP(self.xres.tensor, i * 128,
                              [[T, 128], [128 * T, NCC], [1, 128]])
                st_tok[p] = self.dma(y, f"d_xst{p}", dst,
                                     fma[p][:].rearrange("p (c n) -> p c n", n=128),
                                     waits=ev_tok[p])
                fma_rd[p] = st_tok[p]
                self.xres_tok.append(st_tok[p])
            self.xres_ready = list(st_tok)
            self.wait(y, [tk for tk in st_tok if tk])

    def ln_chunk(self, strm, ncols, dst_fn, strm_tok, dst_waits, tmp):
        sb, ps = self.sb, self.ps
        v, t, s = self.v, self.t, self.s
        NCC = self.NCC
        rb = sb["rowbuf"]
        n = ncols
        sq_tok = [None, None]
        mm_mean = None
        mm_msq = None
        for c in range(NCC):
            p = c % 2
            w = []
            if c == 0:
                w = [strm_tok, self.const_dma_tok] + self.bank_free(7) + self.bank_free(6)
            mm_mean = self.op(t, "pe", lambda c=c: t.matmul(
                ps[7][0:1, 0:n], sb["ones_col"][:],
                strm[:, c * n:(c + 1) * n],
                start=(c == 0), stop=(c == NCC - 1)), waits=w)
            wsq = [strm_tok] + self.tmp_free(p)
            sq_tok[p] = self.op(s, "act", lambda c=c, p=p: s.activation(
                tmp[:, p * n:(p + 1) * n], strm[:, c * n:(c + 1) * n], AF.Square),
                waits=wsq)
            mm_msq = self.op(t, "pe", lambda c=c, p=p: t.matmul(
                ps[6][0:1, 0:n], sb["ones_col"][:], tmp[:, p * n:(p + 1) * n],
                start=(c == 0), stop=(c == NCC - 1)), waits=[sq_tok[p]])
            self.set_tmp_readers(p, [mm_msq])
        e1 = self.op(v, "dve", lambda: v.tensor_copy(rb[0:1, 0:n], ps[7][0:1, 0:n]),
                     waits=[mm_mean])
        e2 = self.op(v, "dve", lambda: v.tensor_copy(rb[0:1, n:2 * n], ps[6][0:1, 0:n]),
                     waits=[mm_msq])
        e3 = self.op(v, "dve", lambda: v.tensor_tensor(
            rb[0:1, 2 * n:3 * n], rb[0:1, 0:n], rb[0:1, 0:n], ALU.mult), waits=[e1])
        e4 = self.op(v, "dve", lambda: v.tensor_tensor(
            rb[0:1, 2 * n:3 * n], rb[0:1, n:2 * n], rb[0:1, 2 * n:3 * n], ALU.subtract),
            waits=[e2, e3])
        e5 = self.op(s, "act", lambda: s.activation(
            rb[0:1, 3 * n:4 * n], rb[0:1, 2 * n:3 * n], AF.Sqrt,
            bias=self.sb["cvals"][0:1, 0:1]),
            waits=[e4])
        e6 = self.op(v, "dve", lambda: v.reciprocal(rb[0:1, 3 * n:4 * n],
                     rb[0:1, 3 * n:4 * n]), waits=[e5])
        bmean = self.op(t, "pe", lambda: t.matmul(
            ps[7][:, 0:n], sb["ones_row_f"][:], rb[0:1, 0:n], start=True, stop=True),
            waits=[e1, e6])
        brstd = self.op(t, "pe", lambda: t.matmul(
            ps[6][:, 0:n], sb["ones_row_f"][:], rb[0:1, 3 * n:4 * n],
            start=True, stop=True), waits=[e6, e2])
        xn_toks = []
        rd_toks = []
        for c in range(NCC):
            p = c % 2
            t1 = self.op(v, "dve", lambda c=c, p=p: v.tensor_tensor(
                tmp[:, p * n:(p + 1) * n], strm[:, c * n:(c + 1) * n],
                ps[7][:, 0:n], ALU.subtract),
                waits=[bmean] + self.tmp_free(p))
            t2 = self.op(v, "dve", lambda c=c, p=p: v.tensor_tensor(
                dst_fn(c), tmp[:, p * n:(p + 1) * n], ps[6][:, 0:n], ALU.mult),
                waits=[t1, brstd] + list(dst_waits))
            self.set_tmp_readers(p, [t2])
            xn_toks.append(t2)
            rd_toks.append(t1)
        self.set_bank_readers(7, rd_toks)
        self.set_bank_readers(6, xn_toks)
        self.ln_readers = [mm_mean, mm_msq, sq_tok[0]] + \
            ([sq_tok[1]] if sq_tok[1] else []) + rd_toks
        return xn_toks, rd_toks

    def tmp_free(self, p):
        if not hasattr(self, "_tmp_rd"):
            self._tmp_rd = {0: [], 1: []}
        return list(self._tmp_rd.get(p, []))

    def set_tmp_readers(self, p, toks):
        if not hasattr(self, "_tmp_rd"):
            self._tmp_rd = {0: [], 1: []}
        self._tmp_rd[p] = list(toks)

    def set_bank_readers(self, bank, toks):
        if not hasattr(self, "_bank_rd"):
            self._bank_rd = {}
        self._bank_rd[bank] = list(toks)

    def bank_free(self, bank):
        if not hasattr(self, "_bank_rd"):
            self._bank_rd = {}
        return list(self._bank_rd.get(bank, []))

    def ffn_phase(self, call):
        nc, sb, ps = self.nc, self.sb, self.ps
        g, v, t, s, y = self.g, self.v, self.t, self.s, self.y
        C, E, I, ISH = self.C, self.E, self.I, self.ISH
        NCC, NI, NISH, NTC, TFN, NSF = (self.NCC, self.NI, self.NISH, self.NTC,
                                        self.TFN, self.NSF)
        NSLOT = self.NSLOT
        SLOTW = max(I, C, ISH)
        self._tmp_rd = {0: [], 1: []}
        with ExitStack() as fx:
            AA = lambda nm, sh, dt: fx.enter_context(
                nc.sbuf_tensor(nm + f"_f{call}", list(sh), dt))
            slots = [AA(f"slot{i}", (128, NCC * SLOTW), BF16) for i in range(NSLOT)]
            strm = [AA(f"strm{i}", (128, NCC * TFN), F32) for i in range(2)]
            xnf = AA("xnf", (128, NCC * TFN), BF16)
            tmp = AA("tmpf", (128, 2 * TFN), F32)
            cbsb = AA("cbsb", (128, E * TFN), BF16)
            hsil = [AA(f"hsil{i}", (128, TFN), BF16) for i in range(2)]
            hupr = [AA(f"hupr{i}", (128, TFN), BF16) for i in range(2)]
            hpr = [AA(f"hpr{i}", (128, TFN), BF16) for i in range(2)]
            sc_tm = AA("sc_tm", (128, NSF * E), F32)
            sc_wk = AA("sc_wk", (128, NSF * E), F32)
            sc_msk = AA("sc_msk", (128, NSF * E), F32)
            sc_tmp = AA("sc_tmp", (128, NSF * E), F32)
            sc_red = AA("sc_red", (128, NSF), F32)
            sc_den = AA("sc_den", (128, NSF), F32)
            cbfm = AA("cbfm", (1, E * TFN), BF16)
            scfm = AA("scfm", (E, TFN), F32)
            nat = ([AA(f"nat{i}", (128, NCC * SLOTW), BF16) for i in range(2)]
                   if call == 0 else None)

            seq = []
            for e in range(E):
                seq += [("g", e), ("u", e), ("d", e)]
            seq += [("sg", -1), ("su", -1), ("sd", -1)]
            NMAT = len(seq)

            slot_ready = [None] * NSLOT
            slot_read = [[] for _ in range(NSLOT)]
            slot_store = [[] for _ in range(NSLOT)]
            nat_read = [[], []]
            if call == 0:
                self.prepass_done = {}

            def scratch_ap(kind, e):
                if kind in ("g", "u", "d"):
                    return self.wT[kind][e]
                return self.wT[kind]

            def nat_src(kind, e):
                if kind == "g":
                    return self.din["w_gate"][e]
                if kind == "u":
                    return self.din["w_up"][e]
                if kind == "d":
                    return self.din["w_down"][e]
                return self.din[{"sg": "shg", "su": "shu", "sd": "shd"}[kind]]

            def mat_dims(kind):
                if kind in ("g", "u"):
                    return NCC, I
                if kind == "d":
                    return NI, C
                if kind in ("sg", "su"):
                    return NCC, ISH
                return NISH, C

            def load_slot(ms):
                si = ms % NSLOT
                kind, e = seq[ms % NMAT]
                tc = ms // NMAT
                rch, cols = mat_dims(kind)
                if call == 0 and tc == 0:
                    pnat = ms % 2
                    src = nat_src(kind, e)
                    ncol = cols // 128
                    ld = self.dma(
                        g, f"d_nat{pnat}",
                        nat[pnat][:, 0:ncol * rch * 128]
                        .rearrange("p (a b) -> p a b", a=ncol),
                        src.rearrange("(a p) b -> p a b", p=128),
                        waits=nat_read[pnat])
                    trt = None
                    for dc in range(rch):
                        for ic in range(ncol):
                            w = [ld]
                            if dc == 0 and ic == 0:
                                w += slot_read[si] + slot_store[si]
                            trt = self.dma_tr(
                                y, f"d_tr{si}",
                                slots[si][:, dc * cols + ic * 128:
                                          dc * cols + (ic + 1) * 128],
                                nat[pnat][:, ic * rch * 128 + dc * 128:
                                          ic * rch * 128 + (dc + 1) * 128],
                                waits=w)
                    nat_read[pnat] = [trt]
                    slot_ready[si] = trt
                    slot_read[si] = []
                    scr = scratch_ap(kind, e)
                    st = None
                    for dc in range(rch):
                        st = self.dma(s, f"d_pst{si}",
                                      scr[dc * 128:(dc + 1) * 128, :],
                                      slots[si][:, dc * cols:(dc + 1) * cols],
                                      waits=[trt])
                    slot_store[si] = [st]
                    self.prepass_done[(kind, e)] = st
                else:
                    w = list(slot_read[si]) + slot_store[si]
                    if call == 0:
                        w.append(self.prepass_done[(kind, e)])
                    scr = scratch_ap(kind, e)
                    slot_ready[si] = self.dma(
                        s, f"d_sl{si}",
                        slots[si][:, 0:rch * cols]
                        .rearrange("p (a b) -> p a b", a=rch),
                        scr.rearrange("(a p) b -> p a b", p=128), waits=w)
                    slot_read[si] = []
                    slot_store[si] = []

            ms_loaded = [0]

            def ensure_loaded(upto):
                lim = min(upto, NTC * NMAT)
                while ms_loaded[0] < lim:
                    load_slot(ms_loaded[0])
                    ms_loaded[0] += 1

            strm_ld = [None, None]
            strm_rd = [[], []]
            strm_st = [None, None]
            pout_readers = []
            xnf_readers = []
            prev_rt = []
            last_hup = []
            tok_dn = None

            for tc in range(NTC):
                p = tc % 2
                t0 = tc * TFN
                w = list(strm_rd[p]) + ([strm_st[p]] if strm_st[p] else [])
                w += self.stream_src_toks(t0, TFN)
                src = bass.AP(self.xres.tensor, t0,
                              [[self.T, 128], [128 * self.T, NCC], [1, TFN]])
                strm_ld[p] = self.dma(y, f"d_rs{p}",
                                      strm[p][:].rearrange("q (a b) -> q a b", a=NCC),
                                      src, waits=w)
                strm_rd[p] = []
                xn_toks, ln_rd = self.ln_chunk(
                    strm[p], TFN, lambda c: xnf[:, c * TFN:(c + 1) * TFN],
                    strm_ld[p], xnf_readers, tmp)
                ln_strm_rd = list(self.ln_readers)
                strm_rd[p] += ln_rd
                xnf_readers = []
                if FFN_TRUNC < 3:
                    continue
                rt = self.router(xnf, sc_tm, sc_wk, sc_msk, sc_tmp, sc_red,
                                 sc_den, cbfm, scfm, cbsb, xn_toks, prev_rt)
                prev_rt = rt["reuse"] + last_hup
                if FFN_TRUNC < 4:
                    continue
                if FFN_TRUNC == 4:
                    ensure_loaded((tc + 1) * NMAT)
                    continue
                dgi = 0
                for em in range(E + 1):
                    shared = (em == E)
                    irange = NISH if shared else NI
                    base_ms = tc * NMAT + 3 * em
                    ensure_loaded(base_ms + 3 + (NSLOT - 3))
                    si_g = base_ms % NSLOT
                    si_u = (base_ms + 1) % NSLOT
                    si_d = (base_ms + 2) % NSLOT
                    sg_tok = slot_ready[si_g]
                    su_tok = slot_ready[si_u]
                    sd_tok = slot_ready[si_d]
                    _, gcols = mat_dims("sg" if shared else "g")
                    _, dcols = mat_dims("sd" if shared else "d")
                    for ic in range(irange):
                        bp = dgi % 2
                        bank = 4 + bp
                        tok_hg = None
                        for dc in range(NCC):
                            w = ([sg_tok, xn_toks[-1]] + self.bank_free(bank)
                                 if dc == 0 else [])
                            fn = lambda dc=dc: t.matmul(
                                ps[bank][:, 0:TFN],
                                slots[si_g][:, dc * gcols + ic * 128:
                                            dc * gcols + ic * 128 + 128],
                                xnf[:, dc * TFN:(dc + 1) * TFN],
                                start=(dc == 0), stop=False)
                            if dc == NCC - 1:
                                tok_hg = self.op(t, "pe", fn, waits=w)
                            else:
                                self.opn(t, fn, waits=w)
                        tok_hu = None
                        for dc in range(NCC):
                            w = [su_tok] if dc == 0 else []
                            fn = lambda dc=dc: t.matmul(
                                ps[bank][:, TFN:2 * TFN],
                                slots[si_u][:, dc * gcols + ic * 128:
                                            dc * gcols + ic * 128 + 128],
                                xnf[:, dc * TFN:(dc + 1) * TFN],
                                start=False, stop=(dc == NCC - 1))
                            if dc == NCC - 1:
                                tok_hu = self.op(t, "pe", fn, waits=w)
                            else:
                                self.opn(t, fn, waits=w)
                        slot_read[si_g].append(tok_hg)
                        slot_read[si_u].append(tok_hu)
                        xnf_readers = [tok_hg, tok_hu]
                        if FFN_TRUNC == 5:
                            self.set_bank_readers(bank, [tok_hu])
                            dgi += 1
                            continue
                        tok_sil = self.op(s, "act", lambda bank=bank, bp=bp: s.activation(
                            hsil[bp][:], ps[bank][:, 0:TFN], AF.Silu),
                            waits=[tok_hu] + self.hbuf_free(bp))
                        if shared:
                            tok_hp = self.op(v, "dve", lambda bank=bank, bp=bp:
                                v.scalar_tensor_tensor(
                                    hpr[bp][:], hsil[bp][:], 0.5,
                                    ps[bank][:, TFN:2 * TFN], ALU.mult, ALU.mult),
                                waits=[tok_sil, tok_hu] + self.hbuf_free(bp))
                            self.set_bank_readers(bank, [tok_hp])
                        else:
                            tok_hup = self.op(v, "dve", lambda bank=bank, bp=bp, em=em:
                                v.tensor_tensor(
                                    hupr[bp][:], ps[bank][:, TFN:2 * TFN],
                                    cbsb[:, em * TFN:(em + 1) * TFN], ALU.mult),
                                waits=[tok_hu, rt["cb"][em]] + self.hbuf_free(bp))
                            last_hup = [tok_hup]
                            tok_hp = self.op(v, "dve", lambda bp=bp: v.tensor_tensor(
                                hpr[bp][:], hupr[bp][:], hsil[bp][:], ALU.mult),
                                waits=[tok_hup, tok_sil])
                            self.set_bank_readers(bank, [tok_sil, tok_hup])
                        if FFN_TRUNC < 7:
                            dgi += 1
                            continue
                        first_of_tc = (em == 0 and ic == 0)
                        last_of_tc = (em == E and ic == irange - 1)
                        tok_dn = None
                        for dc in range(NCC):
                            flat = dc * TFN
                            obank, ooff = flat // 512, flat % 512
                            w = [tok_hp, sd_tok]
                            if first_of_tc and dc == 0:
                                w = w + pout_readers
                            start = first_of_tc and ooff == 0
                            stop = last_of_tc and (ooff + TFN == 512 or dc == NCC - 1)
                            fn = lambda dc=dc, obank=obank, ooff=ooff, start=start, stop=stop: t.matmul(
                                ps[obank][:, ooff:ooff + TFN],
                                slots[si_d][:, ic * dcols + dc * 128:
                                            ic * dcols + dc * 128 + 128],
                                hpr[bp][:], start=start, stop=stop)
                            if dc == NCC - 1:
                                tok_dn = self.op(t, "pe", fn, waits=w)
                            else:
                                self.opn(t, fn, waits=w)
                        slot_read[si_d].append(tok_dn)
                        self.set_hbuf_readers(bp, [tok_dn])
                        dgi += 1
                if FFN_TRUNC < 7:
                    continue
                ep = []
                for dc in range(NCC):
                    flat = dc * TFN
                    obank, ooff = flat // 512, flat % 512
                    ep.append(self.op(v, "dve",
                        lambda dc=dc, obank=obank, ooff=ooff: v.tensor_tensor(
                            strm[p][:, dc * TFN:(dc + 1) * TFN],
                            strm[p][:, dc * TFN:(dc + 1) * TFN],
                            ps[obank][:, ooff:ooff + TFN], ALU.add),
                        waits=[tok_dn, strm_ld[p]] + ln_strm_rd))
                pout_readers = ep
                strm_rd[p] = strm_rd[p] + ep
                dstst = bass.AP(self.xres.tensor, t0,
                                [[self.T, 128], [128 * self.T, NCC], [1, TFN]])
                strm_st[p] = self.dma(y, f"d_ws{p}", dstst,
                                      strm[p][:].rearrange("q (a b) -> q a b", a=NCC),
                                      waits=ep)
                self.note_stream_store(t0, TFN, strm_st[p])
            self.phase_barrier()

    def hbuf_free(self, bp):
        if not hasattr(self, "_hbuf_rd"):
            self._hbuf_rd = {0: [], 1: []}
        return list(self._hbuf_rd.get(bp, []))

    def set_hbuf_readers(self, bp, toks):
        if not hasattr(self, "_hbuf_rd"):
            self._hbuf_rd = {0: [], 1: []}
        self._hbuf_rd[bp] = list(toks)

    def note_stream_store(self, t0, n, tok):
        if not hasattr(self, "_stream_st"):
            self._stream_st = {}
        for off in range(t0, t0 + n, 128):
            self._stream_st[off] = tok

    def stream_src_toks(self, t0, n):
        toks = []
        if hasattr(self, "_stream_st"):
            for off in range(t0, t0 + n, 128):
                if off in self._stream_st:
                    toks.append(self._stream_st[off])
                    continue
                toks += [t for t in getattr(self, "xres_ready", []) if t]
        else:
            toks += [t for t in getattr(self, "xres_ready", []) if t]
        return toks

    def phase_barrier(self):
        toks = [(nm, self.cnt[nm]) for nm in list(self.cnt)]
        for eng in [self.t, self.v, self.s, self.g, self.y]:
            self.wait(eng, toks)

    def router(self, xnf, sc_tm, sc_wk, sc_msk, sc_tmp, sc_red, sc_den,
               cbfm, scfm, cbsb, xn_toks, prev):
        sb, ps = self.sb, self.ps
        v, t, s = self.v, self.t, self.s
        E, NCC, TFN, NSF, KTOP = self.E, self.NCC, self.TFN, self.NSF, self.KTOP
        mz = self.op(v, "dve", lambda: v.memset(sc_msk[:], 0.0), waits=list(prev))
        mm = None
        for c in range(NCC):
            w = [xn_toks[c]]
            if c == 0:
                w += self.bank_free(7) + [self.tok_gwT] + list(prev)
            fn = lambda c=c: t.matmul(
                ps[7][0:E, 0:TFN], sb["gwT"][:, c * E:(c + 1) * E],
                xnf[:, c * TFN:(c + 1) * TFN],
                start=(c == 0), stop=(c == NCC - 1))
            if c == NCC - 1:
                mm = self.op(t, "pe", fn, waits=w)
            else:
                self.opn(t, fn, waits=w)
        sig = self.op(s, "act", lambda: s.activation(
            scfm[:], ps[7][0:E, 0:TFN], AF.Sigmoid), waits=[mm] + list(prev))
        self.set_bank_readers(7, [sig])
        tr_toks = []
        for sf in range(NSF):
            w = [sig, self.t_idf] + (self.bank_free(6) if sf == 0 else [])
            tk = self.op(t, "pe", lambda sf=sf: t.transpose(
                ps[6][0:128, 0:E], scfm[0:E, sf * 128:(sf + 1) * 128],
                sb["ident_f"][0:E, 0:E]), waits=w)
            ev = self.op(v, "dve", lambda sf=sf: v.tensor_copy(
                sc_tm[:, sf * E:(sf + 1) * E], ps[6][0:128, 0:E]), waits=[tk, mz])
            tr_toks.append(ev)
            self.wait(t, [ev])
        self.set_bank_readers(6, tr_toks[-1:])
        v3 = lambda buf: buf[:].rearrange("p (a e) -> p a e", e=E)
        eb3 = sb["ebias_bc"][:].unsqueeze(1).to_broadcast([128, NSF, E])
        e1 = self.op(v, "dve", lambda: v.tensor_tensor(
            v3(sc_wk), v3(sc_tm), eb3, ALU.add),
            waits=tr_toks + [self.tok_consts])
        last = [e1, mz]
        for it in range(KTOP):
            r1 = self.op(v, "dve", lambda: v.tensor_reduce(
                sc_red[:], v3(sc_wk), AX.X, ALU.max), waits=last)
            red3 = sc_red[:].unsqueeze(2).to_broadcast([128, NSF, E])
            r2 = self.op(v, "dve", lambda red3=red3: v.tensor_tensor(
                v3(sc_tmp), v3(sc_wk), red3, ALU.is_ge), waits=[r1])
            r3 = self.op(v, "dve", lambda: v.tensor_tensor(
                sc_msk[:], sc_msk[:], sc_tmp[:], ALU.add), waits=[r2])
            r4 = self.op(v, "dve", lambda: v.scalar_tensor_tensor(
                sc_wk[:], sc_tmp[:], NEG_BIG, sc_wk[:], ALU.mult, ALU.add),
                waits=[r2])
            last = [r3, r4]
        m1 = self.op(v, "dve", lambda: v.tensor_tensor(
            sc_tmp[:], sc_msk[:], sc_tm[:], ALU.mult), waits=last)
        m2 = self.op(v, "dve", lambda: v.tensor_reduce(
            sc_den[:], v3(sc_tmp), AX.X, ALU.add), waits=[m1])
        m3 = self.op(v, "dve", lambda: v.reciprocal(sc_den[:], sc_den[:]),
                     waits=[m2])
        den3 = sc_den[:].unsqueeze(2).to_broadcast([128, NSF, E])
        m4 = self.op(v, "dve", lambda: v.scalar_tensor_tensor(
            v3(sc_tmp), v3(sc_tmp), ROUTE_SCALE * 0.5, den3, ALU.mult, ALU.mult),
            waits=[m3])
        cb_fm_toks = []
        for e in range(E):
            for sf in range(NSF):
                w = [m4, self.t_idf] + self.bank_free(6)
                tk = self.op(t, "pe", lambda e=e, sf=sf: t.transpose(
                    ps[6][0:1, 0:128], sc_tmp[:, sf * E + e: sf * E + e + 1],
                    sb["ident_f"][:]), waits=w)
                ev = self.op(s, "act", lambda e=e, sf=sf: s.activation(
                    cbfm[0:1, e * TFN + sf * 128: e * TFN + (sf + 1) * 128],
                    ps[6][0:1, 0:128], AF.Copy), waits=[tk] + list(prev))
                cb_fm_toks.append(ev)
                self.set_bank_readers(6, [ev])
                self.wait(t, [ev])
        cb_toks = [None] * E
        for e0 in range(0, E, 2):
            w = [cb_fm_toks[-1]] + self.bank_free(6)
            both = min(2, E - e0)
            lastmm = None
            for k in range(both):
                fn = lambda e0=e0, k=k, both=both: t.matmul(
                    ps[6][:, k * TFN:(k + 1) * TFN], sb["ones_row_b"][:],
                    cbfm[0:1, (e0 + k) * TFN:(e0 + k + 1) * TFN],
                    start=(k == 0), stop=(k == both - 1))
                if k == both - 1:
                    lastmm = self.op(t, "pe", fn, waits=w if k == 0 else [])
                else:
                    self.opn(t, fn, waits=w)
            ev = self.op(s, "act", lambda e0=e0, both=both: s.activation(
                cbsb[:, e0 * TFN:(e0 + both) * TFN], ps[6][:, 0:both * TFN],
                AF.Copy), waits=[lastmm] + list(prev))
            for k in range(both):
                cb_toks[e0 + k] = ev
            self.set_bank_readers(6, [ev])
        return dict(cb=cb_toks, reuse=[m4, cb_toks[-1]])

    def attn_phase(self):
        self.phase_barrier()
        self.attn_emit()
        self.phase_barrier()

    def attn_emit(self):
        nc, sb, ps = self.nc, self.sb, self.ps
        g, v, t, s, y = self.g, self.v, self.t, self.s, self.y
        C, E, HS, DM, DD = self.C, self.E, self.HS, self.DM, self.DD
        NCC, TAT, NB, NTS, NQ, LBLK = (self.NCC, self.TAT, self.NB, self.NTS,
                                       self.NQ, self.LBLK)
        H, HPC = self.H, self.HPC
        BPB = 512 // HS
        self._tmp_rd = {0: [], 1: []}
        with ExitStack() as axx:
            AA = lambda nm, sh, dt: axx.enter_context(
                nc.sbuf_tensor(nm + "_a", list(sh), dt))
            aslot = [AA(f"aslot{i}", (128, NCC * C), BF16) for i in range(2)]
            strm = AA("strma", (128, NCC * TAT), F32)
            xna = AA("xna", (128, NCC * (TAT + 1)), BF16)
            xxb = AA("xxb", (128, NCC * TAT), BF16)
            xqA = AA("xqA", (128, NCC * TAT), BF16)
            xqB = AA("xqB", (128, NCC * TAT), BF16)
            mixb = AA("mixb", (96, 2 * TAT), BF16)
            midsb = AA("midsb", (DD, TAT), BF16)
            tmp = AA("tmpa", (128, 2 * TAT), F32)
            ash = AA("ash", (128, NCC * TAT), BF16)
            ainv = AA("ainv", (128, NCC * TAT), BF16)
            ggp = AA("ggp", (128, NCC * TAT), BF16)
            ewt = AA("ewt", (128, TAT), F32)
            ddt = AA("ddt", (128, TAT), F32)
            aat = AA("aat", (128, TAT), F32)
            grt = AA("grt", (128, TAT), F32)
            xet = AA("xet", (128, TAT), F32)
            rbc = AA("rbc", (128, TAT), BF16)
            kkc = AA("kkc", (128, TAT), BF16)
            ktc = AA("ktc", (128, TAT), BF16)
            rukc = AA("rukc", (128, TAT), BF16)
            rrfm = AA("rrfm", (128, NCC * TAT), BF16)
            vtm = AA("vtm", (128, NTS * C), BF16)
            kttm = AA("kttm", (128, NTS * C), BF16)
            attb = AA("attb", (128, H * NTS * LBLK), BF16)
            sbf = AA("sbf", (128, NCC * HS), BF16)
            ssb = AA("ssb", (128, NCC * HS), F32)
            sba = AA("sba", (128, NB * NCC * HS), BF16)
            abl = AA("abl", (128, NCC * NB), F32)
            dfm = AA("dfm", (HPC, NCC * TAT), BF16)
            dtm = AA("dtm", (128, NTS * H), BF16)
            ybuf = AA("ybuf", (128, C), F32)
            ynrm = AA("ynrm", (128, C), F32)
            gn16 = AA("gn16", (128, 8 * H), F32)
            gts = AA("gts", (128, C), BF16)
            zbt = AA("zbt", (128, C), BF16)
            zfm = AA("zfm", (128, NCC * TAT), BF16)

            s0a = self.op(v, "dve", lambda: v.memset(ssb[:], 0.0))
            s0b = self.op(v, "dve", lambda: v.memset(sbf[:], 0.0))
            bnd = []
            for c in range(NCC):
                bnd.append(self.op(v, "dve", lambda c=c: v.memset(
                    xna[:, c * (TAT + 1):c * (TAT + 1) + 1], 0.0)))
            xna_rd = list(bnd)
            sbf_tok = s0b
            ssb_tok = s0a

            wli = [0]
            aslot_rd = [[], []]
            aslot_tok = [None, None]

            def load_w(name):
                si = wli[0] % 2
                tk = self.dma(g, f"d_aw{si}",
                              aslot[si][:].rearrange("p (a b) -> p a b", a=NCC),
                              self.din[name].rearrange("(a p) b -> p a b", p=128),
                              waits=aslot_rd[si])
                aslot_rd[si] = []
                aslot_tok[si] = tk
                wli[0] += 1
                return si

            strm_rd = []
            strm_st = None
            kttm_rd = []
            vtm_rd = []
            attb_rd = []
            rrfm_rd = []
            dfm_rd = []
            zfm_rd = []
            gts_rd = []
            dtm_rd = []

            for q in range(NQ):
                t0 = q * TAT
                si_r = load_w("Wr")
                si_k = load_w("Wk")
                w = list(strm_rd) + ([strm_st] if strm_st else [])
                w += self.stream_src_toks(t0, TAT)
                src = bass.AP(self.xres.tensor, t0,
                              [[self.T, 128], [128 * self.T, NCC], [1, TAT]])
                strm_ld = self.dma(y, "d_ras", strm[:].rearrange("q (a b) -> q a b", a=NCC),
                                   src, waits=w)
                strm_rd = []
                xn_toks, ln_rd = self.ln_chunk(
                    strm, TAT,
                    lambda c: xna[:, c * (TAT + 1) + 1: c * (TAT + 1) + 1 + TAT],
                    strm_ld, xna_rd, tmp)
                ln_strm_rd = list(self.ln_readers)
                strm_rd += ln_rd
                xna_rd = []
                xx_toks = []
                xxx_toks = []
                for c in range(NCC):
                    o = c * (TAT + 1)
                    tkx = self.op(v, "dve", lambda c=c, o=o: v.tensor_tensor(
                        xxb[:, c * TAT:(c + 1) * TAT], xna[:, o:o + TAT],
                        xna[:, o + 1:o + 1 + TAT], ALU.subtract),
                        waits=[xn_toks[c]])
                    xx_toks.append(tkx)
                    tk3 = self.op(v, "dve", lambda c=c, o=o: v.scalar_tensor_tensor(
                        xqA[:, c * TAT:(c + 1) * TAT],
                        xxb[:, c * TAT:(c + 1) * TAT],
                        sb["maacol"][:, 0 * NCC + c:0 * NCC + c + 1],
                        xna[:, o + 1:o + 1 + TAT], ALU.mult, ALU.add),
                        waits=[tkx, self.const_dma_tok])
                    xxx_toks.append(tk3)
                xna_rd += xx_toks
                nbnd = []
                for c in range(NCC):
                    o = c * (TAT + 1)
                    nbnd.append(self.op(v, "dve", lambda c=c, o=o: v.tensor_copy(
                        xna[:, o:o + 1], xna[:, o + TAT:o + TAT + 1]),
                        waits=[xx_toks[c]]))
                mix_toks = []
                mixb_gate = list(getattr(self, "_mixb_rd", []))
                for si5 in range(5):
                    bnk = si5 % 2
                    rowb = (si5 % 3) * 32
                    mmt = None
                    for c in range(NCC):
                        w = [xxx_toks[c]]
                        if c == 0:
                            w += self.bank_free(bnk) + [self.tok_consts]
                        fn = lambda c=c, si5=si5, bnk=bnk, rowb=rowb: t.matmul(
                            ps[bnk][rowb:rowb + DM, 0:TAT],
                            sb["w1sb"][:, c * DM * 5 + si5 * DM: c * DM * 5 + (si5 + 1) * DM],
                            xqA[:, c * TAT:(c + 1) * TAT],
                            start=(c == 0), stop=(c == NCC - 1))
                        if c == NCC - 1:
                            mmt = self.op(t, "pe", fn, waits=w)
                        else:
                            self.opn(t, fn, waits=w)
                    ev = self.op(s, "act", lambda si5=si5, bnk=bnk, rowb=rowb: s.activation(
                        mixb[rowb:rowb + DM, (si5 // 3) * TAT:(si5 // 3) * TAT + TAT],
                        ps[bnk][rowb:rowb + DM, 0:TAT],
                        AF.Tanh), waits=[mmt] + mixb_gate)
                    mix_toks.append(ev)
                    self.set_bank_readers(bnk, [ev])

                if ATTN_TRUNC < 2:
                    continue

                def make_xq(si5, dstbuf, extra_waits):
                    maa_idx = {0: 1, 1: 2, 2: 3, 3: 4, 4: 5}[si5]
                    toks = []
                    for c in range(NCC):
                        bnk = 2 + (c % 2)
                        rowb = (si5 % 3) * 32
                        w = [mix_toks[si5]] + self.bank_free(bnk)
                        mmt = self.op(t, "pe", lambda c=c, si5=si5, bnk=bnk, rowb=rowb: t.matmul(
                            ps[bnk][:, 0:TAT],
                            sb["w2sb"][rowb:rowb + DM, (si5 // 3) * C + c * 128:
                                       (si5 // 3) * C + (c + 1) * 128],
                            mixb[rowb:rowb + DM, (si5 // 3) * TAT:(si5 // 3) * TAT + TAT],
                            start=True, stop=True), waits=w)
                        p = c % 2
                        t1 = self.op(v, "dve", lambda c=c, bnk=bnk, p=p: v.scalar_tensor_tensor(
                            tmp[:, p * TAT:(p + 1) * TAT], ps[bnk][:, 0:TAT],
                            sb["maacol"][:, maa_idx * NCC + c: maa_idx * NCC + c + 1],
                            xxb[:, c * TAT:(c + 1) * TAT], ALU.add, ALU.mult),
                            waits=[mmt, xx_toks[c]] + self.tmp_free(p))
                        o = c * (TAT + 1)
                        t2 = self.op(v, "dve", lambda c=c, o=o, p=p, dstbuf=dstbuf: v.tensor_tensor(
                            dstbuf[:, c * TAT:(c + 1) * TAT],
                            tmp[:, p * TAT:(p + 1) * TAT],
                            xna[:, o + 1:o + 1 + TAT], ALU.add),
                            waits=[t1] + list(extra_waits))
                        self.set_tmp_readers(p, [t2])
                        self.set_bank_readers(bnk, [t1])
                        toks.append(t2)
                    return toks

                xqw_toks = make_xq(0, xqA, mix_toks)
                xna_rd += xqw_toks
                mmt = None
                for c in range(NCC):
                    w = [xqw_toks[c]] + (self.bank_free(4) if c == 0 else [])
                    fn = lambda c=c: t.matmul(
                        ps[4][0:DD, 0:TAT], sb["dw1sb"][:, c * DD:(c + 1) * DD],
                        xqA[:, c * TAT:(c + 1) * TAT],
                        start=(c == 0), stop=(c == NCC - 1))
                    if c == NCC - 1:
                        mmt = self.op(t, "pe", fn, waits=w)
                    else:
                        self.opn(t, fn, waits=w)
                mid_tok = self.op(s, "act", lambda: s.activation(
                    midsb[:], ps[4][0:DD, 0:TAT], AF.Tanh),
                    waits=[mmt] + list(getattr(self, "_midsb_rd", [])))
                self.set_bank_readers(4, [mid_tok])
                ash_toks = [None] * NCC
                ainv_toks = [None] * NCC
                gg_toks = [None] * NCC
                abl_toks = [None] * NCC
                prev_ew = []
                for oc in range(NCC):
                    bnk = 2 + (oc % 2)
                    w = [mid_tok] + self.bank_free(bnk)
                    mmt = self.op(t, "pe", lambda oc=oc, bnk=bnk: t.matmul(
                        ps[bnk][:, 0:TAT], sb["dw2sb"][0:DD, oc * 128:(oc + 1) * 128],
                        midsb[:], start=True, stop=True), waits=w)
                    e_ew = self.op(s, "act", lambda oc=oc, bnk=bnk: s.activation(
                        ewt[:], ps[bnk][:, 0:TAT], AF.Exp,
                        bias=sb["tdcol"][:, oc:oc + 1]), waits=[mmt] + prev_ew)
                    self.set_bank_readers(bnk, [e_ew])
                    e_dd = self.op(s, "act", lambda: s.activation(
                        ddt[:], ewt[:], AF.Exp, scale=-1.0), waits=[e_ew])
                    e_xe = self.op(s, "act", lambda: s.activation(
                        xet[:], ewt[:], AF.Exp, scale=1.0), waits=[e_ew])
                    prev_ew = [e_dd, e_xe]
                    sc_toks = []
                    for b in range(NB):
                        sl = slice(b * LBLK, (b + 1) * LBLK)
                        z64 = sb["zcol"][:].to_broadcast([128, LBLK])
                        t_a = self.op(v, "dve", lambda sl=sl, z64=z64: v.tensor_tensor_scan(
                            aat[:, sl], ddt[:, sl], z64, 1.0, ALU.mult, ALU.add),
                            waits=[e_dd] + ([] if oc == 0 and b == 0 else []))
                        rsl_in = ddt[:, sl][:, LBLK - 1::-1]
                        rsl_out = grt[:, sl][:, LBLK - 1::-1]
                        t_g = self.op(v, "dve", lambda rsl_in=rsl_in, rsl_out=rsl_out, z64=z64:
                            v.tensor_tensor_scan(rsl_out, rsl_in, z64, 1.0,
                                                 ALU.mult, ALU.add), waits=[e_dd])
                        sc_toks += [t_a, t_g]
                    ainv_toks[oc] = self.op(v, "dve", lambda oc=oc: v.reciprocal(
                        ainv[:, oc * TAT:(oc + 1) * TAT], aat[:]), waits=sc_toks)
                    ash_toks[oc] = self.op(v, "dve", lambda oc=oc: v.tensor_tensor(
                        ash[:, oc * TAT:(oc + 1) * TAT], aat[:], xet[:], ALU.mult),
                        waits=sc_toks + [e_xe])
                    gg_toks[oc] = self.op(v, "dve", lambda oc=oc: v.tensor_tensor(
                        ggp[:, oc * TAT:(oc + 1) * TAT], grt[:], xet[:], ALU.mult),
                        waits=sc_toks + [e_xe])
                    abl_toks[oc] = self.op(v, "dve", lambda oc=oc: v.tensor_copy(
                        abl[:, oc * NB:(oc + 1) * NB], aat[:, LBLK - 1::LBLK]),
                        waits=sc_toks)
                    prev_ew += [ainv_toks[oc], ash_toks[oc], gg_toks[oc], abl_toks[oc]]

                if ATTN_TRUNC < 3:
                    continue
                xqk_toks = make_xq(1, xqA, [mmt, mid_tok])
                xqr_toks = make_xq(3, xqB, [])
                xna_rd += xqk_toks + xqr_toks

                rr_toks = [None] * NCC
                att_toks = []
                kt_tr = [None] * NCC
                dfm_toks = [None] * NCC
                b_pt = [6, 7]
                rb_tok = None
                kk_tok = None
                kt_tok = None
                ruk_tok = None
                dmm_prev = None
                mm1_prev = None
                for oc in range(NCC):
                    w0 = [aslot_tok[si_r]] + self.bank_free(2)
                    mmr = None
                    for c in range(NCC):
                        w = (w0 + [xqr_toks[c]]) if c == 0 else [xqr_toks[c]]
                        fn = lambda c=c, oc=oc: t.matmul(
                            ps[2][:, 0:TAT],
                            aslot[si_r][:, c * C + oc * 128: c * C + oc * 128 + 128],
                            xqB[:, c * TAT:(c + 1) * TAT],
                            start=(c == 0), stop=(c == NCC - 1))
                        if c == NCC - 1:
                            mmr = self.op(t, "pe", fn, waits=w)
                        else:
                            self.opn(t, fn, waits=w)
                    aslot_rd[si_r].append(mmr)
                    rb_tok = self.op(s, "act", lambda: s.activation(
                        rbc[:], ps[2][:, 0:TAT], AF.Copy),
                        waits=[mmr] + ([ruk_tok] if ruk_tok else []))
                    rr_toks[oc] = self.op(v, "dve", lambda oc=oc: v.tensor_tensor(
                        rrfm[:, oc * TAT:(oc + 1) * TAT], ps[2][:, 0:TAT],
                        ash[:, oc * TAT:(oc + 1) * TAT], ALU.mult),
                        waits=[mmr, ash_toks[oc]] + rrfm_rd)
                    rrfm_rd = []
                    self.set_bank_readers(2, [rb_tok, rr_toks[oc]])
                    w0 = [aslot_tok[si_k]] + self.bank_free(3)
                    mmk = None
                    for c in range(NCC):
                        w = (w0 + [xqk_toks[c]]) if c == 0 else [xqk_toks[c]]
                        fn = lambda c=c, oc=oc: t.matmul(
                            ps[3][:, 0:TAT],
                            aslot[si_k][:, c * C + oc * 128: c * C + oc * 128 + 128],
                            xqA[:, c * TAT:(c + 1) * TAT],
                            start=(c == 0), stop=(c == NCC - 1))
                        if c == NCC - 1:
                            mmk = self.op(t, "pe", fn, waits=w)
                        else:
                            self.opn(t, fn, waits=w)
                    aslot_rd[si_k].append(mmk)
                    kk_tok = self.op(v, "dve", lambda oc=oc: v.tensor_tensor(
                        kkc[:], ps[3][:, 0:TAT], ainv[:, oc * TAT:(oc + 1) * TAT],
                        ALU.mult), waits=[mmk, ainv_toks[oc]] +
                        ([mm1_prev] if mm1_prev else []))
                    kt_tok = self.op(v, "dve", lambda oc=oc: v.tensor_tensor(
                        ktc[:], ps[3][:, 0:TAT], ggp[:, oc * TAT:(oc + 1) * TAT],
                        ALU.mult), waits=[mmk, gg_toks[oc]] +
                        ([kt_tr[oc - 1]] if oc > 0 and kt_tr[oc - 1] else []))
                    ruk_tok = self.op(v, "dve", lambda oc=oc: v.scalar_tensor_tensor(
                        rukc[:], ps[3][:, 0:TAT], sb["ucol"][:, oc:oc + 1], rbc[:],
                        ALU.mult, ALU.mult),
                        waits=[mmk, rb_tok] + ([dmm_prev] if dmm_prev else []))
                    self.set_bank_readers(3, [kk_tok, kt_tok, ruk_tok])
                    if PH1_TRUNC < 2:
                        continue
                    w = [ruk_tok, self.t_o2] + self.bank_free(5)
                    dmm = self.op(t, "pe", lambda: t.matmul(
                        ps[5][0:HPC, 0:TAT], sb["ones2_b"][:], rukc[:],
                        start=True, stop=True), waits=w)
                    dmm_prev = dmm
                    dfm_toks[oc] = self.op(s, "act", lambda oc=oc: s.activation(
                        dfm[0:HPC, oc * TAT:(oc + 1) * TAT], ps[5][0:HPC, 0:TAT],
                        AF.Copy), waits=[dmm] + dfm_rd)
                    dfm_rd = []
                    self.set_bank_readers(5, [dfm_toks[oc]])
                    if PH1_TRUNC < 3:
                        continue
                    ktr = None
                    for ts in range(NTS):
                        w = [kt_tok] + (kttm_rd if oc == 0 and ts == 0 else [])
                        ktr = self.dma_tr(y, "d_ktt",
                                          kttm[:, ts * C + oc * 128: ts * C + (oc + 1) * 128],
                                          ktc[:, ts * 128:(ts + 1) * 128], waits=w)
                    kt_tr[oc] = ktr
                    kttm_rd = []
                    if PH1_TRUNC < 4:
                        continue
                    bpt = b_pt[oc % 2]
                    nmm = 0
                    mm1 = None
                    m1_order = [(hh, b) for pb in range(2) for hh in range(HPC)
                                for b in range(NB) if b % 2 == pb]
                    for hh, b in m1_order:
                            first = (m1_order[0] == (hh, b) or
                                     (b % 2 == 1 and (hh, b) == next(
                                         x for x in m1_order if x[1] % 2 == 1)))
                            last = ((hh, b) == m1_order[-1] or
                                    (b % 2 == 0 and (hh, b) == [x for x in m1_order
                                                                if x[1] % 2 == 0][-1]))
                            w = [kk_tok, rr_toks[oc]] + (self.bank_free(bpt) if nmm == 0 else [])
                            fn = lambda hh=hh, b=b, oc=oc, bpt=bpt, first=first, last=last: t.matmul(
                                ps[bpt][(b % 2) * 64:(b % 2) * 64 + 64,
                                        (hh * NTS + b // 2) * LBLK:
                                        (hh * NTS + b // 2) * LBLK + LBLK],
                                kkc[hh * HS:(hh + 1) * HS, b * LBLK:(b + 1) * LBLK],
                                rrfm[hh * HS:(hh + 1) * HS,
                                     oc * TAT + b * LBLK: oc * TAT + (b + 1) * LBLK],
                                start=first, stop=last)
                            if last:
                                mm1 = self.op(t, "pe", fn, waits=w)
                            else:
                                self.opn(t, fn, waits=w)
                            nmm += 1
                    if PH1_TRUNC < 5:
                        mm1_prev = mm1
                        continue
                    for hh in range(HPC):
                        for pb in range(2):
                            h = oc * HPC + hh
                            m3 = sb["mask_b"][pb * 64:pb * 64 + 64, :] \
                                .unsqueeze(1).to_broadcast([64, NTS, LBLK])
                            w = [mm1, self.t_msk] + (attb_rd if oc == 0 else [])
                            tk = self.op(v, "dve", lambda hh=hh, pb=pb, h=h, bpt=bpt, m3=m3:
                                v.tensor_tensor(
                                    attb[pb * 64:pb * 64 + 64,
                                         h * NTS * LBLK:(h + 1) * NTS * LBLK]
                                    .rearrange("p (a b) -> p a b", b=LBLK),
                                    ps[bpt][pb * 64:pb * 64 + 64,
                                            hh * NTS * LBLK:(hh + 1) * NTS * LBLK]
                                    .rearrange("p (a b) -> p a b", b=LBLK),
                                    m3, ALU.mult), waits=w)
                            att_toks.append(tk)
                    attb_rd = [] if oc == 0 else attb_rd
                    self.set_bank_readers(bpt, att_toks[-2 * HPC:])
                    mm1_prev = mm1

                if ATTN_TRUNC < 4:
                    continue
                xqv_toks = make_xq(2, xqA, [mmk, kk_tok, kt_tok, ruk_tok])
                xna_rd += xqv_toks
                si_v = load_w("Wv")
                v_toks = []
                W5 = min(512, C)
                for ts in range(NTS):
                    for nh in range(max(1, C // 512)):
                        bnk = 2 + (nh % 2)
                        mmv = None
                        for c in range(NCC):
                            w = [xqv_toks[c], aslot_tok[si_v]]
                            if c == 0:
                                w += self.bank_free(bnk)
                            fn = lambda c=c, ts=ts, nh=nh, bnk=bnk: t.matmul(
                                ps[bnk][:, 0:W5],
                                xqA[:, c * TAT + ts * 128: c * TAT + (ts + 1) * 128],
                                aslot[si_v][:, c * C + nh * W5: c * C + nh * W5 + W5],
                                start=(c == 0), stop=(c == NCC - 1))
                            if c == NCC - 1:
                                mmv = self.op(t, "pe", fn, waits=w)
                            else:
                                self.opn(t, fn, waits=w)
                        aslot_rd[si_v].append(mmv)
                        ev = self.op(s, "act", lambda ts=ts, nh=nh, bnk=bnk: s.activation(
                            vtm[:, ts * C + nh * W5: ts * C + nh * W5 + W5],
                            ps[bnk][:, 0:W5], AF.Copy),
                            waits=[mmv] + (vtm_rd if ts == 0 and nh == 0 else []))
                        v_toks.append(ev)
                        self.set_bank_readers(bnk, [ev])
                vtm_rd = []
                xqg_toks = make_xq(4, xqB, [mmr, rr_toks[NCC - 1], ruk_tok])
                xna_rd += xqg_toks

                if ATTN_TRUNC < 5:
                    continue
                sU = [None, None]
                for b in range(NB):
                    bnk = 2 + (b % 2)
                    nmm = 0
                    mmu = None
                    u_order = [h for hp in range(2) for h in range(H) if h % 2 == hp]
                    ev_last = [h for h in u_order if h % 2 == 0][-1]
                    od_first = next(h for h in u_order if h % 2 == 1)
                    for h in u_order:
                        first = (h == u_order[0] or h == od_first)
                        last = (h == ev_last or h == u_order[-1])
                        w = [kt_tr[h // HPC], v_toks[-1]]
                        if nmm == 0:
                            w += self.bank_free(bnk)
                        fn = lambda h=h, b=b, bnk=bnk, first=first, last=last: t.matmul(
                            ps[bnk][(h % 2) * 64:(h % 2) * 64 + 64,
                                    (h // 2) * HS:(h // 2) * HS + HS],
                            kttm[(b % 2) * 64:(b % 2) * 64 + 64,
                                 (b // 2) * C + h * HS:(b // 2) * C + (h + 1) * HS],
                            vtm[(b % 2) * 64:(b % 2) * 64 + 64,
                                (b // 2) * C + h * HS:(b // 2) * C + (h + 1) * HS],
                            start=first, stop=last)
                        if h == u_order[-1]:
                            mmu = self.op(t, "pe", fn, waits=w)
                        else:
                            self.opn(t, fn, waits=w)
                        nmm += 1
                    snap = self.op(v, "dve", lambda b=b: v.tensor_copy(
                        sba[:, b * NCC * HS:(b + 1) * NCC * HS], sbf[:]),
                        waits=[sbf_tok])
                    abl3 = abl[:, b::NB].unsqueeze(2).to_broadcast([128, NCC, HS])
                    ssb3 = ssb[:].rearrange("p (a b) -> p a b", b=HS)
                    smul = self.op(v, "dve", lambda abl3=abl3, ssb3=ssb3: v.tensor_tensor(
                        ssb3, ssb3, abl3, ALU.mult),
                        waits=[ssb_tok] + [tk for tk in abl_toks if tk])
                    sadd = self.op(v, "dve", lambda bnk=bnk: v.tensor_tensor(
                        ssb[:], ssb[:], ps[bnk][:, 0:NCC * HS], ALU.add),
                        waits=[smul, mmu])
                    sbf_tok = self.op(v, "dve", lambda: v.tensor_copy(sbf[:], ssb[:]),
                                      waits=[sadd, snap])
                    ssb_tok = sadd
                    self.set_bank_readers(bnk, [sadd])
                    sU[b % 2] = sadd

                if ATTN_TRUNC < 6:
                    continue
                si_g5 = load_w("Wg")
                zfm_toks = []
                for ts in range(NTS):
                    g_toks = []
                    for nh in range(max(1, C // 512)):
                        bnk = 2 + (nh % 2)
                        mmg = None
                        for c in range(NCC):
                            w = [xqg_toks[c], aslot_tok[si_g5]]
                            if c == 0:
                                w += self.bank_free(bnk)
                            fn = lambda c=c, ts=ts, nh=nh, bnk=bnk: t.matmul(
                                ps[bnk][:, 0:W5],
                                xqB[:, c * TAT + ts * 128: c * TAT + (ts + 1) * 128],
                                aslot[si_g5][:, c * C + nh * W5: c * C + nh * W5 + W5],
                                start=(c == 0), stop=(c == NCC - 1))
                            if c == NCC - 1:
                                mmg = self.op(t, "pe", fn, waits=w)
                            else:
                                self.opn(t, fn, waits=w)
                        aslot_rd[si_g5].append(mmg)
                        ev = self.op(s, "act", lambda nh=nh, bnk=bnk: s.activation(
                            gts[:, nh * W5:nh * W5 + W5], ps[bnk][:, 0:W5],
                            AF.Silu), waits=[mmg] + (gts_rd if nh == 0 else []))
                        gts_rd = []
                        g_toks.append(ev)
                        self.set_bank_readers(bnk, [ev])
                    nyb = max(1, C // 512)
                    order = []
                    for b in (2 * ts, 2 * ts + 1):
                        for h in range(H):
                            for which in ("m3", "m4"):
                                yb = (h * HS) // 512
                                order.append((b, h, which, yb, (yb, b % 2)))
                    firsts = {}
                    lasts = {}
                    bank_first = {}
                    bank_last = {}
                    for idx, (b, h, which, yb, key) in enumerate(order):
                        if key not in firsts:
                            firsts[key] = idx
                        lasts[key] = idx
                        if yb not in bank_first:
                            bank_first[yb] = idx
                        bank_last[yb] = idx
                    mmy = None
                    ylast = {}
                    for idx, (b, h, which, yb, key) in enumerate(order):
                        first = (firsts[key] == idx)
                        last = (lasts[key] == idx)
                        w = []
                        if bank_first[yb] == idx:
                            w += self.bank_free(yb)
                        if which == "m3":
                            w += [att_toks[-1], v_toks[-1]]
                            fn = lambda b=b, h=h, yb=yb, ts=ts, first=first, last=last: t.matmul(
                                ps[yb][(b % 2) * 64:(b % 2) * 64 + 64,
                                       (h * HS) % 512:(h * HS) % 512 + HS],
                                attb[(b % 2) * 64:(b % 2) * 64 + 64,
                                     (h * NTS + ts) * LBLK:(h * NTS + ts + 1) * LBLK],
                                vtm[(b % 2) * 64:(b % 2) * 64 + 64,
                                    ts * C + h * HS: ts * C + (h + 1) * HS],
                                start=first, stop=last)
                        else:
                            w += [sbf_tok, rr_toks[h // HPC]]
                            fn = lambda b=b, h=h, yb=yb, first=first, last=last: t.matmul(
                                ps[yb][(b % 2) * 64:(b % 2) * 64 + 64,
                                       (h * HS) % 512:(h * HS) % 512 + HS],
                                rrfm[(h % 2) * 64:(h % 2) * 64 + 64,
                                     (h // 2) * TAT + b * LBLK:
                                     (h // 2) * TAT + (b + 1) * LBLK],
                                sba[(h % 2) * 64:(h % 2) * 64 + 64,
                                    b * NCC * HS + (h // 2) * HS:
                                    b * NCC * HS + (h // 2 + 1) * HS],
                                start=first, stop=last)
                        if bank_last[yb] == idx:
                            ylast[yb] = self.op(t, "pe", fn, waits=w)
                            mmy = ylast[yb]
                        else:
                            self.opn(t, fn, waits=w)
                    dt_toks = []
                    for oc in range(NCC):
                        w = [dfm_toks[oc], self.t_idb] + self.bank_free(4)
                        p4b = ps[4][:].bitcast(BF16)
                        tk = self.op(t, "pe", lambda oc=oc, ts=ts, p4b=p4b: t.transpose(
                            p4b[0:128, 0:HPC],
                            dfm[0:HPC, oc * TAT + ts * 128: oc * TAT + ts * 128 + 128],
                            sb["ident_b"][0:HPC, 0:HPC]), waits=w)
                        ev = self.op(v, "dve", lambda oc=oc, ts=ts, p4b=p4b: v.tensor_copy(
                            dtm[:, ts * H + oc * HPC: ts * H + (oc + 1) * HPC],
                            p4b[0:128, 0:HPC]),
                            waits=[tk] + (dtm_rd if ts == 0 and oc == 0 else []))
                        dt_toks.append(ev)
                        self.set_bank_readers(4, [ev])
                        self.wait(t, [ev])
                    dtm_rd = []
                    d3 = dtm[:, ts * H:(ts + 1) * H].unsqueeze(2).to_broadcast([128, H, HS])
                    v3 = vtm[:, ts * C:(ts + 1) * C].rearrange("p (a b) -> p a b", b=HS)
                    y2t = self.op(v, "dve", lambda d3=d3, v3=v3: v.tensor_tensor(
                        ynrm[:].rearrange("p (a b) -> p a b", b=HS), v3, d3, ALU.mult),
                        waits=dt_toks + [v_toks[-1]])
                    ye_toks = []
                    for yb in range(nyb):
                        n0 = yb * 512
                        n1 = min(C, n0 + 512)
                        ye_toks.append(self.op(v, "dve", lambda yb=yb, n0=n0, n1=n1: v.tensor_tensor(
                            ybuf[:, n0:n1], ps[yb][:, 0:n1 - n0], ynrm[:, n0:n1],
                            ALU.add), waits=[y2t, ylast[yb]]))
                    self.set_bank_readers(0, ye_toks[:1])
                    if nyb > 1:
                        self.set_bank_readers(1, ye_toks[1:])
                    y3 = ybuf[:].rearrange("p (a b) -> p a b", b=HS)
                    r1 = self.op(v, "dve", lambda y3=y3: v.tensor_reduce(
                        gn16[:, 0:H], y3, AX.X, ALU.add), waits=ye_toks)
                    sq1 = self.op(s, "act", lambda: s.activation(
                        ynrm[:], ybuf[:], AF.Square), waits=ye_toks + [y2t])
                    r2 = self.op(v, "dve", lambda: v.tensor_reduce(
                        gn16[:, H:2 * H], ynrm[:].rearrange("p (a b) -> p a b", b=HS),
                        AX.X, ALU.add), waits=[sq1])
                    g1 = self.op(v, "dve", lambda: v.tensor_scalar_mul(
                        gn16[:, 2 * H:3 * H], gn16[:, 0:H], 1.0 / HS), waits=[r1])
                    g2 = self.op(v, "dve", lambda: v.tensor_scalar_mul(
                        gn16[:, 3 * H:4 * H], gn16[:, H:2 * H], 1.0 / HS), waits=[r2])
                    g3 = self.op(v, "dve", lambda: v.tensor_tensor(
                        gn16[:, 4 * H:5 * H], gn16[:, 2 * H:3 * H],
                        gn16[:, 2 * H:3 * H], ALU.mult), waits=[g1])
                    g4 = self.op(v, "dve", lambda: v.tensor_tensor(
                        gn16[:, 5 * H:6 * H], gn16[:, 3 * H:4 * H],
                        gn16[:, 4 * H:5 * H], ALU.subtract), waits=[g2, g3])
                    g5 = self.op(s, "act", lambda: s.activation(
                        gn16[:, 6 * H:7 * H], gn16[:, 5 * H:6 * H], AF.Sqrt,
                        bias=sb["cvals"][:, 1:2]), waits=[g4])
                    g6 = self.op(v, "dve", lambda: v.reciprocal(
                        gn16[:, 7 * H:8 * H], gn16[:, 6 * H:7 * H]), waits=[g5])
                    m3b = gn16[:, 2 * H:3 * H].unsqueeze(2).to_broadcast([128, H, HS])
                    rs3 = gn16[:, 7 * H:8 * H].unsqueeze(2).to_broadcast([128, H, HS])
                    yn3 = ynrm[:].rearrange("p (a b) -> p a b", b=HS)
                    n1t = self.op(v, "dve", lambda y3=y3, m3b=m3b, yn3=yn3: v.tensor_tensor(
                        yn3, y3, m3b, ALU.subtract), waits=[g1, sq1, r2] + ye_toks)
                    n2t = self.op(v, "dve", lambda y3=y3, rs3=rs3, yn3=yn3: v.tensor_tensor(
                        y3, yn3, rs3, ALU.mult), waits=[n1t, g6])
                    n3t = self.op(v, "dve", lambda: v.tensor_tensor(
                        ynrm[:], ybuf[:], sb["lnw_bc"][:], ALU.mult),
                        waits=[n2t, self.tok_consts])
                    n4t = self.op(v, "dve", lambda: v.tensor_tensor(
                        ybuf[:], ynrm[:], sb["lnb_bc"][:], ALU.add), waits=[n3t])
                    zt = self.op(v, "dve", lambda: v.tensor_tensor(
                        zbt[:], ybuf[:], gts[:], ALU.mult),
                        waits=[n4t] + g_toks + zfm_toks[-NCC:])
                    gts_rd = [zt]
                    for oc in range(NCC):
                        w = [zt] + (zfm_rd if ts == 0 and oc == 0 else [])
                        ztr = self.dma_tr(y, "d_ztr",
                                          zfm[:, oc * TAT + ts * 128: oc * TAT + (ts + 1) * 128],
                                          zbt[:, oc * 128:(oc + 1) * 128], waits=w)
                        zfm_toks.append(ztr)
                    zfm_rd = []

                if ATTN_TRUNC < 7:
                    continue
                si_o = load_w("Wo")
                ep_toks = []
                for oc in range(NCC):
                    bnk = 2 + (oc % 2)
                    mmo = None
                    for c in range(NCC):
                        w = [aslot_tok[si_o], zfm_toks[-1]]
                        if c == 0:
                            w += self.bank_free(bnk)
                        fn = lambda c=c, oc=oc, bnk=bnk: t.matmul(
                            ps[bnk][:, 0:TAT],
                            aslot[si_o][:, c * C + oc * 128: c * C + oc * 128 + 128],
                            zfm[:, c * TAT:(c + 1) * TAT],
                            start=(c == 0), stop=(c == NCC - 1))
                        if c == NCC - 1:
                            mmo = self.op(t, "pe", fn, waits=w)
                        else:
                            self.opn(t, fn, waits=w)
                    aslot_rd[si_o].append(mmo)
                    ep = self.op(v, "dve", lambda oc=oc, bnk=bnk: v.tensor_tensor(
                        strm[:, oc * TAT:(oc + 1) * TAT],
                        strm[:, oc * TAT:(oc + 1) * TAT],
                        ps[bnk][:, 0:TAT], ALU.add), waits=[mmo, strm_ld] + ln_strm_rd)
                    ep_toks.append(ep)
                    self.set_bank_readers(bnk, [ep])
                strm_rd += ep_toks
                dstst = bass.AP(self.xres.tensor, t0,
                                [[self.T, 128], [128 * self.T, NCC], [1, TAT]])
                strm_st = self.dma(y, "d_was", dstst,
                                   strm[:].rearrange("q (a b) -> q a b", a=NCC),
                                   waits=ep_toks)
                self.note_stream_store(t0, TAT, strm_st)
                self._mixb_rd = [mmy]
                self._midsb_rd = [mmy]
                xna_rd += nbnd
                kttm_rd = [sU[0], sU[1]] if sU[0] else []
                vtm_rd = [mmy, y2t]
                attb_rd = [mmy]
                rrfm_rd = [mmy]
                dfm_rd = dt_toks
                dtm_rd = [y2t]
                zfm_rd = [mmo]
                strm_rd = strm_rd + [strm_st]
            if ATTN_TRUNC >= 7:
                self.wait(y, [strm_st])
            else:
                self.phase_barrier()


    def xout_phase(self):
        sb, ps = self.sb, self.ps
        g, v, t, s, y = self.g, self.v, self.t, self.s, self.y
        T, C, NCC = self.T, self.C, self.NCC
        with self.nc.sbuf_tensor("ostg0", [128, C], F32) as s0, \
             self.nc.sbuf_tensor("ostg1", [128, C], F32) as s1, \
             self.nc.sbuf_tensor("ofm0", [128, NCC * 128], F32) as f0, \
             self.nc.sbuf_tensor("ofm1", [128, NCC * 128], F32) as f1:
            stg = [s0, s1]
            fma = [f0, f1]
            ld = [None, None]
            st = [None, None]
            trd = [[], []]
            ntile = T // 128
            final = None
            for i in range(ntile):
                p = i % 2
                w = self.stream_src_toks(i * 128, 128) + list(trd[p])
                src = bass.AP(self.xres.tensor, i * 128,
                              [[T, 128], [128 * T, NCC], [1, 128]])
                ld[p] = self.dma(y, f"d_ol{p}",
                                 fma[p][:].rearrange("q (a b) -> q a b", a=NCC),
                                 src, waits=w)
                trd[p] = []
                evs = []
                for c in range(NCC):
                    bank = 6 + (c % 2)
                    wl = [ld[p], self.t_idf] + self.bank_free(bank)
                    if st[p] is not None:
                        wl.append(st[p])
                    tk = self.op(t, "pe", lambda c=c, bank=bank: t.transpose(
                        ps[bank][:, 0:128], fma[p][:, c * 128:(c + 1) * 128],
                        sb["ident_f"][:]), waits=wl)
                    ev = self.op(v, "dve", lambda c=c, bank=bank: v.tensor_copy(
                        stg[p][:, c * 128:(c + 1) * 128], ps[bank][:, 0:128]),
                        waits=[tk])
                    self.set_bank_readers(bank, [ev])
                    evs.append(ev)
                    trd[p].append(tk)
                st[p] = self.dma(y, f"d_os{p}", self.out[i * 128:(i + 1) * 128, :],
                                 stg[p][:], waits=evs)
                final = st[p]
            self.wait(y, [st[0] or final, st[1] or final])


def make_in_maps(cfg, inputs):
    T, C = cfg["T"], cfg["C"]
    B = inputs["L0_attn_x"].shape[0]
    m = {
        "maa_x": "L0_attn_time_maa_x", "maa_w": "L0_attn_time_maa_w",
        "maa_k": "L0_attn_time_maa_k", "maa_v": "L0_attn_time_maa_v",
        "maa_r": "L0_attn_time_maa_r", "maa_g": "L0_attn_time_maa_g",
        "maa_w1": "L0_attn_time_maa_w1", "maa_w2": "L0_attn_time_maa_w2",
        "tdecay": "L0_attn_time_decay", "tdw1": "L0_attn_time_decay_w1",
        "tdw2": "L0_attn_time_decay_w2", "faaaa": "L0_attn_time_faaaa",
        "Wr": "L0_attn_Wr", "Wk": "L0_attn_Wk", "Wv": "L0_attn_Wv",
        "Wg": "L0_attn_Wg", "Wo": "L0_attn_Wo",
        "ln_w": "L0_attn_ln_w", "ln_b": "L0_attn_ln_b",
        "gate_w": "L0_ffn_gate_w", "ebias": "L0_ffn_expert_bias",
        "w_gate": "L0_ffn_w_gate", "w_up": "L0_ffn_w_up", "w_down": "L0_ffn_w_down",
        "shg": "L0_ffn_shared_gate_w", "shu": "L0_ffn_shared_up_w",
        "shd": "L0_ffn_shared_down_w",
    }
    shared = {k: np.ascontiguousarray(inputs[vk], dtype=np.float32) for k, vk in m.items()}
    maps = []
    for b in range(B):
        d = dict(shared)
        d["x"] = np.ascontiguousarray(inputs["L0_attn_x"][b], dtype=np.float32)
        maps.append(d)
    return maps


def kernel(**inputs):
    cfg = full_cfg()
    b = Builder(cfg)
    nc = b.build()
    in_maps = make_in_maps(cfg, inputs)
    res = run_bass_kernel_spmd(nc, in_maps, core_ids=list(range(len(in_maps))))
    outs = [np.asarray(r["out"], dtype=np.float32) for r in res.results]
    return np.stack(outs, axis=0)
